# version 13
# baseline (speedup 1.0000x reference)
import numpy as np
from collections import defaultdict
from contextlib import ExitStack

import concourse.bass as bass
import concourse.mybir as mybir
from concourse.bass_utils import run_bass_kernel_spmd

F32 = mybir.dt.float32
BF16 = mybir.dt.bfloat16
AF = mybir.ActivationFunctionType
ALU = mybir.AluOpType
AX = mybir.AxisListType

NCORE = 8
TCORE = 2048
NPASS = 2
TP = 1024
NN = 512
H, S, FF = 4096, 1024, 9984
HC, FC = 32, 78
JG = [10] * 6 + [9] * 2
NH = 8
RC_H = [JG[h] + 7 + 4 + (1 if h == 0 else 0) for h in range(NH)]
ROWOFF = [0]
for _h in range(NH):
    ROWOFF.append(ROWOFF[-1] + RC_H[_h] * 128)
EPS = 1e-5
ENGS = ("tensor", "vector", "scalar", "gpsimd", "sync")


def _build_program(debug=False):
    nc = bass.Bass("TRN2", target_bir_lowering=False)

    x_d = nc.dram_tensor("x", [TCORE, H], F32, kind="ExternalInput").ap()
    proj1_d = nc.dram_tensor("proj1", [17, 2, 128, 16, 128], F32,
                             kind="ExternalInput").ap()
    wg_d = nc.dram_tensor("wg2", [2, 128, 16, 8], F32, kind="ExternalInput").ap()
    gu_d = nc.dram_tensor("gu", [FC, 4, 128, 8, 256], F32, kind="ExternalInput").ap()
    OW_TOT = sum(32 * 128 * RC_H[h] * 128 for h in range(NH))
    OWOFF = [0]
    for h in range(NH):
        OWOFF.append(OWOFF[-1] + 32 * 128 * RC_H[h] * 128)
    outw_d = nc.dram_tensor("outw", [OW_TOT], F32, kind="ExternalInput").ap()
    bt_d = nc.dram_tensor("bt", [128, 32], F32, kind="ExternalInput").ap()
    lpt_d = nc.dram_tensor("lpt", [128, 8], F32, kind="ExternalInput").ap()
    df_d = nc.dram_tensor("df", [1, 6], F32, kind="ExternalInput").ap()
    p8_d = nc.dram_tensor("p8", [8, 128], F32, kind="ExternalInput").ap()
    id_d = nc.dram_tensor("ident", [128, 128], F32, kind="ExternalInput").ap()
    out_d = nc.dram_tensor("out", [TCORE, H], F32, kind="ExternalOutput").ap()
    if debug:
        u_dd = nc.dram_tensor("u_sc", [S, TCORE], BF16, kind="ExternalOutput").ap()
        dl_dd = nc.dram_tensor("dl_sc", [S, TCORE], F32, kind="ExternalOutput").ap()
        acc_dd = nc.dram_tensor("acc_sc", [H, TCORE], F32, kind="ExternalOutput").ap()
        hwdbg = nc.dram_tensor("hw_dbg", [128, TCORE], BF16, kind="ExternalOutput").ap()
        xndbg = nc.dram_tensor("xn_dbg", [H, TCORE], BF16, kind="ExternalOutput").ap()
        statdbg = nc.dram_tensor("stat_dbg", [128, 128], F32, kind="ExternalOutput").ap()
    else:
        u_dd = nc.dram_tensor("u_sc", [S, TCORE], BF16).ap()
        dl_dd = nc.dram_tensor("dl_sc", [S, TCORE], F32).ap()
        acc_dd = nc.dram_tensor("acc_sc", [H, TCORE], F32).ap()

    st = ExitStack()
    SB = lambda n, shp, dt: st.enter_context(nc.sbuf_tensor(n, shp, dt))
    PS_ = lambda n, shp: st.enter_context(nc.psum_tensor(n, shp, F32))
    sems = {}

    def SEM(n):
        sems[n] = st.enter_context(nc.semaphore(n))

    with st:
        xn_sb = SB("xn", [128, HC, TP], BF16)
        hw_sb = SB("hw", [128, TP], BF16)
        V_sb = SB("V", [128, 7, TP], BF16)
        hstg = SB("hstg", [128, TP], F32)
        edf_bc = SB("edf", [128, 8], F32)
        cpool = SB("cpool", [128, 4], F32)
        ones1 = SB("ones1", [1, 128], F32)
        b_sb = SB("btsb", [128, 32], F32)
        lp_sb = SB("lptsb", [128, 8], F32)
        df_sb = SB("dfsb", [1, 8], F32)
        edf_row = SB("edfrow", [1, 8], F32)
        p8_sb = SB("p8sb", [8, 128], F32)
        id_sb = SB("idsb", [128, 128], F32)
        xh = [SB(f"xh{i}", [128, 2048], F32) for i in range(2)]
        stat = SB("stat", [128, 8], F32)
        p1b = [SB(f"p1b{i}", [128, 16, 128], BF16) for i in range(2)]
        ust0 = SB("ust0", [128, TP], BF16)
        ust = [ust0, ust0]
        lgt_sb = SB("lgt", [8, TP], F32)
        lgt_tm = SB("lgttm", [128, 64], F32)
        w_tm = SB("wtm", [128, 64], F32)
        rt1 = SB("rt1", [128, 64], F32)
        rt2 = SB("rt2", [128, 64], F32)
        rts = SB("rts", [128, 40], F32)
        gub = [SB(f"gub{i}", [128, 8, 256], BF16) for i in range(2)]
        owb = [[SB(f"owb{i}{j}", [128, 11, 128], BF16) for j in range(2)]
               for i in range(2)]
        usl = SB("usl", [128, TP], BF16)
        dsl = SB("dsl", [128, TP], F32)
        vscr = SB("vscr", [128, 2048], F32)
        vs = [bass.AP(vscr, 0, [[2048, 128], [1, TP]]),
              bass.AP(vscr, TP, [[2048, 128], [1, TP]])]
        dst = [vs[0], vs[1]]
        route_fm = lgt_sb
        silg = [SB(f"silg{i}", [128, NN], F32) for i in range(2)]
        P_sb = SB("Psb", [128, 10, TP], BF16)
        stg = [SB(f"stgo{i}", [128, TP], F32) for i in range(2)]
        accb = [SB(f"accb{i}", [128, 16, 64], F32) for i in range(2)]
        out_sb = SB("outsb", [128, H], F32)

        ps = [PS_(f"ps{i}", [128, TP]) for i in range(4)]

        for n in ("s_ini", "s_ims", "s_edf", "s_x0", "s_x1", "s_sqa", "s_var",
                  "s_rstd", "s_xtm", "s_tp", "s_xnf", "s_p10", "s_p11",
                  "s_p1c", "s_s2mm", "s_s2ev", "s_ud0", "s_ud1", "s_dd0",
                  "s_dd1", "s_hev", "s_lev", "s_rtp", "s_r1", "s_r2", "s_r3",
                  "s_r4", "s_r5", "s_rbp", "s_r6", "s_rbc", "s_hw", "s_gu0",
                  "s_gu1", "s_guc", "s_gG0", "s_gG1", "s_gU0", "s_sil",
                  "s_pv", "s_vb", "s_uds", "s_dds", "s_ow00", "s_ow01", "s_ow10", "s_ow11",
                  "s_owc0", "s_mg", "s_oev", "s_a40", "s_a41", "s_4tp",
                  "s_4ev", "s_out", "s_vch", "s_sch", "s_xnfA", "s_xnfD"):
            SEM(n)
        for m in range(32):
            SEM(f"s_ac{m}")

        C = defaultdict(int)
        prog = {e: [] for e in ENGS}

        def E(eng, f):
            prog[eng].append(f)

        def W(eng, sem, val):
            if val > 0:
                E(eng, lambda e, s=sems[sem], v=val: e.wait_ge(s, v))

        def INC(sem, n=1):
            C[sem] += n
            return C[sem]

        def VCH(fn):
            v = INC("s_vch")
            E("vector", lambda e, f=fn: f(e).then_inc(sems["s_vch"], 1))
            W("vector", "s_vch", v)

        def SCH(fn):
            v = INC("s_sch")
            E("scalar", lambda e, f=fn: f(e).then_inc(sems["s_sch"], 1))
            W("scalar", "s_sch", v)

        for dst_, src_ in ((b_sb[:, :], bt_d), (lp_sb[:, :], lpt_d),
                           (df_sb[0:1, 0:6], df_d), (p8_sb[:, :], p8_d),
                           (id_sb[:, :], id_d)):
            W('sync', "s_ini", C["s_ini"])
            INC("s_ini", 16)
            E("sync", lambda e, d=dst_, s=src_: e.dma_start(d, s)
              .then_inc(sems["s_ini"], 16))
        for col, val in ((0, EPS), (1, -0.5), (2, -1.0)):
            E("gpsimd", lambda e, c=col, v=val: e.memset(cpool[:, c:c + 1], v))
        E("gpsimd", lambda e: e.memset(ones1[:, :], 1.0))
        INC("s_ims")
        E("gpsimd", lambda e: e.engine_nop().then_inc(sems["s_ims"], 1))
        W("scalar", "s_ini", C["s_ini"])
        W("scalar", "s_ims", C["s_ims"])
        INC("s_edf")
        E("scalar", lambda e: e.activation(edf_row[0:1, 0:6], df_sb[0:1, 0:6],
                                           AF.Exp, scale=cpool[0:1, 2:3])
          .then_inc(sems["s_edf"], 1))
        W("tensor", "s_edf", C["s_edf"])
        INC("s_edf")
        E("tensor", lambda e: e.matmul(ps[0][:, 0:6], ones1[0:1, :],
                                       edf_row[0:1, 0:6], start=True, stop=True)
          .then_inc(sems["s_edf"], 1))
        W("vector", "s_edf", C["s_edf"])
        INC("s_edf")
        E("vector", lambda e: e.tensor_copy(edf_bc[:, 0:6], ps[0][:, 0:6])
          .then_inc(sems["s_edf"], 1))

        gj = 0
        gm = 0
        gpiece = 0
        gtile = 0
        ghalf = 0
        gxl = [0, 0]
        gtp = 0
        deferred_acc = []
        n_acc = [0] * 32
        stg_last = [None, None]
        pending_acc = []
        stg_used = [(0, 0), (0, 0)]
        stg_xtm = [0, 0]
        tpq = [0, 0]

        for p in range(NPASS):
            def emit_s1_guards(pp):
                if pp > 0:
                    W("scalar", "s_mg", gm)
                    W("vector", "s_mg", gm)
                    W("scalar", "s_vb", NH * pp)
                    W("vector", "s_vb", NH * pp)
                    W("sync", "s_ac30", n_acc[30])
                    W("sync", "s_ac31", n_acc[31])
                    W("sync", "s_vb", NH * pp)
                    W("tensor", "s_oev", C["s_oev"])

            def emit_s1_tile(pp, t):
                nonlocal ghalf, gtp, gtile
                r0 = pp * TP + t * 128
                for q in range(4):
                    uv, us = stg_used[q % 2]
                    W("sync", "s_vch", uv)
                    W("sync", "s_sqa", us)
                    W('sync', f"s_x{q % 2}", C[f"s_x{q % 2}"])
                    INC(f"s_x{q % 2}", 16)
                    E("sync", lambda e, rr=r0, qq=q:
                      e.dma_start(stg[qq % 2][:, :],
                                  x_d[rr:rr + 128, qq * TP:(qq + 1) * TP])
                      .then_inc(sems[f"s_x{qq % 2}"], 16))
                    W("vector", f"s_x{q % 2}", C[f"s_x{q % 2}"])
                    if q == 0:
                        VCH(lambda e, qq=q: e.reduce_sum(
                            stat[:, 0:1], stg[qq % 2][:, :], axis=AX.X))
                    else:
                        VCH(lambda e, qq=q: e.reduce_sum(
                            stat[:, 1:2], stg[qq % 2][:, :], axis=AX.X))
                        VCH(lambda e: e.tensor_add(stat[:, 0:1], stat[:, 0:1],
                                                   stat[:, 1:2]))
                    W("scalar", f"s_x{q % 2}", C[f"s_x{q % 2}"])
                    W("scalar", "s_vch", C["s_vch"])
                    INC("s_sqa")
                    E("scalar", lambda e, qq=q:
                      e.activation(stg[qq % 2][:, :], stg[qq % 2][:, :], AF.Square,
                                   accum_out=stat[:, 2:3] if qq == 0
                                   else stat[:, 3:4])
                      .then_inc(sems["s_sqa"], 1))
                    if q > 0:
                        W("vector", "s_sqa", C["s_sqa"])
                        VCH(lambda e: e.tensor_add(stat[:, 2:3], stat[:, 2:3],
                                                   stat[:, 3:4]))
                    stg_used[q % 2] = (C["s_vch"], C["s_sqa"])
                VCH(lambda e: e.tensor_scalar_mul(stat[:, 4:5], stat[:, 0:1], 1.0 / H))
                VCH(lambda e: e.tensor_mul(stat[:, 7:8], stat[:, 4:5], stat[:, 4:5]))
                INC("s_var")
                E("vector", lambda e: e.scalar_tensor_tensor(
                    stat[:, 5:6], stat[:, 2:3], 1.0 / H, stat[:, 7:8],
                    ALU.mult, ALU.subtract).then_inc(sems["s_var"], 1))
                W("scalar", "s_var", C["s_var"])
                SCH(lambda e: e.activation(stat[:, 6:7], stat[:, 5:6],
                                           AF.Ln, bias=cpool[:, 0:1]))
                INC("s_rstd")
                E("scalar", lambda e: e.activation(stat[:, 6:7], stat[:, 6:7],
                                                   AF.Exp, scale=cpool[:, 1:2])
                  .then_inc(sems["s_rstd"], 1))
                if debug:
                    W("sync", "s_rstd", C["s_rstd"])
                    W('sync', "s_ini", C["s_ini"])
                    INC("s_ini", 16)
                    E("sync", lambda e, pt=p * 8 + t: e.dma_start(
                        statdbg[:, pt * 8:(pt + 1) * 8], stat[:, :])
                      .then_inc(sems["s_ini"], 16))
                for q in range(4):
                    vslot = q % 2
                    W("sync", "s_tp", tpq[vslot])
                    W('sync', f"s_x{q % 2}", C[f"s_x{q % 2}"])
                    INC(f"s_x{q % 2}", 16)
                    E("sync", lambda e, rr=r0, qq=q, vs_=vslot:
                      e.dma_start(vscr[:, vs_ * TP:(vs_ + 1) * TP],
                                  x_d[rr:rr + 128, qq * TP:(qq + 1) * TP])
                      .then_inc(sems[f"s_x{qq % 2}"], 16))
                    W("vector", f"s_x{q % 2}", C[f"s_x{q % 2}"])
                    W("vector", "s_rstd", C["s_rstd"])
                    INC("s_xtm")
                    E("vector", lambda e, vs_=vslot:
                      e.tensor_scalar(vscr[:, vs_ * TP:(vs_ + 1) * TP],
                                      vscr[:, vs_ * TP:(vs_ + 1) * TP],
                                      stat[:, 4:5], stat[:, 6:7],
                                      ALU.subtract, ALU.mult)
                      .then_inc(sems["s_xtm"], 1))
                    for g in range(2):
                        slot = (q * 2 + g) % 2
                        pslot = ps[0][:, slot * 512:slot * 512 + 512]
                        W("tensor", "s_xtm", C["s_xtm"])
                        sl_sem = "s_xnfA" if slot == 0 else "s_xnfD"
                        if C[sl_sem] > 0:
                            W("tensor", sl_sem, C[sl_sem])
                        else:
                            W("tensor", "s_edf", 3)
                        for i in range(4):
                            last = (i == 3)
                            if last:
                                INC("s_tp")
                            def tpf(e, vs_=vslot, gg=g, ii=i, ll=last, psl=pslot):
                                ins = e.transpose(
                                    psl[:, ii * 128:(ii + 1) * 128],
                                    vscr[:, vs_ * TP + (gg * 4 + ii) * 128:
                                         vs_ * TP + (gg * 4 + ii + 1) * 128],
                                    id_sb[:, :])
                                if ll:
                                    ins.then_inc(sems["s_tp"], 1)
                            E("tensor", tpf)
                        gtp = C["s_tp"]
                        kb = q * 8 + g * 4
                        eng = "scalar" if slot == 0 else "vector"
                        sem_e = "s_xnfA" if slot == 0 else "s_xnfD"
                        W(eng, "s_tp", gtp)
                        INC(sem_e)
                        if eng == "scalar":
                            E(eng, lambda e, kb_=kb, tt=t, so=slot, se=sem_e:
                              e.copy(xn_sb[:, kb_:kb_ + 4, tt * 128:(tt + 1) * 128],
                                     bass.AP(ps[0], so * 512,
                                             [[1024, 128], [128, 4], [1, 128]]))
                              .then_inc(sems[se], 1))
                        else:
                            E(eng, lambda e, kb_=kb, tt=t, so=slot, se=sem_e:
                              e.tensor_copy(
                                  xn_sb[:, kb_:kb_ + 4, tt * 128:(tt + 1) * 128],
                                  bass.AP(ps[0], so * 512,
                                          [[1024, 128], [128, 4], [1, 128]]))
                              .then_inc(sems[se], 1))
                    tpq[vslot] = C["s_tp"]
                gtile += 1

            if p == 0:
                emit_s1_guards(0)
                for t in range(8):
                    emit_s1_tile(0, t)

            if debug:
                W("sync", "s_xnfA", C["s_xnfA"])
                W("sync", "s_xnfD", C["s_xnfD"])
                W('sync', "s_ini", C["s_ini"])
                INC("s_ini", 16)
                E("sync", lambda e, pp=p: e.dma_start(
                    xndbg[:, pp * TP:(pp + 1) * TP]
                    .rearrange("(kc p) t -> p kc t", p=128),
                    xn_sb[:, :, :]).then_inc(sems["s_ini"], 16))

            colw = [128] * 17 + [8]
            coff = [c * 128 for c in range(17)] + [2176]
            for c in range(18):
                for q in range(2):
                    if p * 18 + c >= 1:
                        if q == 0:
                            W("gpsimd", "s_p1c", p * 18 + c)
                        else:
                            W("gpsimd", "s_s2mm", p * 18 + c)
                    W('gpsimd', f"s_p1{q}", C[f"s_p1{q}"])
                    INC(f"s_p1{q}", 16)
                    if c < 17:
                        E("gpsimd", lambda e, qq=q, cc=c:
                          e.dma_start(p1b[qq][:, :, 0:128], proj1_d[cc, qq])
                          .then_inc(sems[f"s_p1{qq}"], 16))
                    else:
                        E("gpsimd", lambda e, qq=q:
                          e.dma_start(p1b[qq][:, :, 0:8], wg_d[qq])
                          .then_inc(sems[f"s_p1{qq}"], 16))
            for c in range(18):
                cw = colw[c]
                slot = ps[1] if c % 2 == 0 else ps[2]
                if p * 18 + c >= 2:
                    W("tensor", "s_s2ev", p * 18 + c - 1)
                elif p == 0 and c == 0:
                    W("tensor", "s_xnfA", C["s_xnfA"])
                    W("tensor", "s_xnfD", C["s_xnfD"])
                if p > 0 and c == 0:
                    W("tensor", "s_4ev", C["s_4ev"])
                    W("tensor", "s_xnfA", C["s_xnfA"])
                    W("tensor", "s_xnfD", C["s_xnfD"])
                for q in range(2):
                    W("tensor", f"s_p1{q}", 16 * (p * 18 + c + 1))
                    for k in range(16):
                        for nn in range(2):
                            startf = (q == 0 and k == 0)
                            stopf = (q == 1 and k == 15)
                            i0 = (q == 0 and k == 15 and nn == 1)
                            i1 = (stopf and nn == 1)
                            if i0:
                                INC("s_p1c")
                            if i1:
                                INC("s_s2mm")
                            def s2mm(e, qq=q, kk=k, nnn=nn, sf=startf, pf=stopf,
                                     l0=i0, l1=i1, sl=slot, ww=cw):
                                o = sl[0:ww, nnn * NN:nnn * NN + NN] if ww < 128 \
                                    else sl[:, nnn * NN:nnn * NN + NN]
                                ins = e.matmul(o, p1b[qq][:, kk, 0:ww],
                                               xn_sb[:, qq * 16 + kk,
                                                     nnn * NN:(nnn + 1) * NN],
                                               start=sf, stop=pf)
                                if l0:
                                    ins.then_inc(sems["s_p1c"], 1)
                                elif l1:
                                    ins.then_inc(sems["s_s2mm"], 1)
                            E("tensor", s2mm)
                W("vector", "s_s2mm", C["s_s2mm"])
                if c < 8:
                    par = c % 2
                    if p * 8 + c >= 1:
                        W("vector", "s_ud0", C["s_ud0"])
                        W("vector", "s_ud1", C["s_ud1"])
                    INC("s_s2ev")
                    E("vector", lambda e, pr=par, sl=slot:
                      e.tensor_copy(ust[pr][:, :], sl[:, :])
                      .then_inc(sems["s_s2ev"], 1))
                    W("scalar", "s_s2ev", C["s_s2ev"])
                    W('scalar', f"s_ud{par}", C[f"s_ud{par}"])
                    INC(f"s_ud{par}", 16)
                    E("scalar", lambda e, cc=c, pr=par, pp=p:
                      e.dma_start(u_dd[cc * 128:(cc + 1) * 128,
                                       pp * TP:(pp + 1) * TP], ust[pr][:, :])
                      .then_inc(sems[f"s_ud{pr}"], 16))
                elif c < 16:
                    sc = c - 8
                    par = sc % 2
                    if p * 8 + sc >= 2:
                        W("vector", f"s_dd{par}", C[f"s_dd{par}"])
                    INC("s_s2ev")
                    E("vector", lambda e, pr=par, sl=slot, ss=sc:
                      e.tensor_scalar(dst[pr][:, :], sl[:, :],
                                      lp_sb[:, ss:ss + 1], None, ALU.subtract)
                      .then_inc(sems["s_s2ev"], 1))
                    W("scalar", "s_s2ev", C["s_s2ev"])
                    W('scalar', f"s_dd{par}", C[f"s_dd{par}"])
                    INC(f"s_dd{par}", 16)
                    E("scalar", lambda e, ss=sc, pr=par, pp=p:
                      e.dma_start(dl_dd[ss * 128:(ss + 1) * 128,
                                        pp * TP:(pp + 1) * TP], dst[pr][:, :])
                      .then_inc(sems[f"s_dd{pr}"], 16))
                elif c == 16:
                    INC("s_s2ev")
                    INC("s_hev")
                    E("vector", lambda e, sl=slot:
                      e.tensor_copy(hstg[:, :], sl[:, :])
                      .then_inc(sems["s_hev"], 1))
                    E("vector", lambda e:
                      e.engine_nop().then_inc(sems["s_s2ev"], 1))
                else:
                    INC("s_s2ev")
                    INC("s_lev")
                    E("vector", lambda e, sl=slot:
                      e.tensor_copy(lgt_sb[0:8, :], sl[0:8, :])
                      .then_inc(sems["s_lev"], 1))
                    E("vector", lambda e:
                      e.engine_nop().then_inc(sems["s_s2ev"], 1))
            W("tensor", "s_lev", C["s_lev"])
            W("tensor", "s_hev", C["s_hev"])
            for q in range(8):
                last = (q == 7)
                if last:
                    INC("s_rtp")
                def rtf(e, qq=q, ll=last):
                    ins = e.transpose(ps[1][0:128, qq * 8:(qq + 1) * 8],
                                      lgt_sb[0:8, qq * 128:(qq + 1) * 128],
                                      id_sb[0:8, 0:8])
                    if ll:
                        ins.then_inc(sems["s_rtp"], 1)
                E("tensor", rtf)
            lg3 = bass.AP(lgt_tm, 0, [[64, 128], [8, 8], [1, 8]])
            m1b = bass.AP(rts, 0, [[40, 128], [1, 8], [0, 8]])
            m2b = bass.AP(rts, 8, [[40, 128], [1, 8], [0, 8]])
            rdb = bass.AP(rts, 32, [[40, 128], [1, 8], [0, 8]])
            r13 = bass.AP(rt1, 0, [[64, 128], [8, 8], [1, 8]])
            r23 = bass.AP(rt2, 0, [[64, 128], [8, 8], [1, 8]])
            w3 = bass.AP(w_tm, 0, [[64, 128], [8, 8], [1, 8]])
            m1o = bass.AP(rts, 0, [[40, 128], [1, 8]])
            m2o = bass.AP(rts, 8, [[40, 128], [1, 8]])
            W("vector", "s_rtp", C["s_rtp"])
            VCH(lambda e: e.tensor_copy(lgt_tm[:, :], ps[1][:, 0:64]))
            VCH(lambda e: e.reduce_max(m1o, lg3, axis=AX.X))
            VCH(lambda e: e.tensor_tensor(r13, lg3, m1b, ALU.is_equal))
            VCH(lambda e: e.scalar_tensor_tensor(r23, r13, -1e30, lg3,
                                                 ALU.mult, ALU.add))
            VCH(lambda e: e.reduce_max(m2o, r23, axis=AX.X))
            INC("s_r1")
            E("vector", lambda e: e.tensor_sub(rts[:, 16:24], rts[:, 8:16],
                                               rts[:, 0:8])
              .then_inc(sems["s_r1"], 1))
            W("scalar", "s_r1", C["s_r1"])
            INC("s_r2")
            E("scalar", lambda e: e.activation(rts[:, 16:24], rts[:, 16:24],
                                               AF.Exp).then_inc(sems["s_r2"], 1))
            W("vector", "s_r2", C["s_r2"])
            VCH(lambda e: e.tensor_scalar_add(rts[:, 24:32],
                                              rts[:, 16:24], 1.0))
            VCH(lambda e: e.reciprocal(rts[:, 32:40], rts[:, 24:32]))
            VCH(lambda e: e.tensor_tensor(r23, lg3, m2b, ALU.is_ge))
            INC("s_r3")
            E("vector", lambda e: e.tensor_tensor(r13, lg3, m1b, ALU.subtract)
              .then_inc(sems["s_r3"], 1))
            W("scalar", "s_r3", C["s_r3"])
            INC("s_r4")
            E("scalar", lambda e: e.activation(rt1[:, :], rt1[:, :], AF.Exp)
              .then_inc(sems["s_r4"], 1))
            W("vector", "s_r4", C["s_r4"])
            VCH(lambda e: e.tensor_tensor(w3, r13, r23, ALU.mult))
            INC("s_r5")
            E("vector", lambda e: e.tensor_tensor(w3, w3, rdb, ALU.mult)
              .then_inc(sems["s_r5"], 1))
            W("tensor", "s_r5", C["s_r5"])
            for q in range(8):
                last = (q == 7)
                if last:
                    INC("s_rbp")
                def rbf(e, qq=q, ll=last):
                    ins = e.transpose(ps[3][0:8, qq * 128:(qq + 1) * 128],
                                      w_tm[:, qq * 8:(qq + 1) * 8], id_sb[:, :])
                    if ll:
                        ins.then_inc(sems["s_rbp"], 1)
                E("tensor", rbf)
            W("vector", "s_rbp", C["s_rbp"])
            INC("s_r6")
            E("vector", lambda e: e.tensor_copy(route_fm[0:8, :], ps[3][0:8, :])
              .then_inc(sems["s_r6"], 1))
            W("tensor", "s_r6", C["s_r6"])
            for nn in range(2):
                last = (nn == 1)
                if last:
                    INC("s_rbc")
                def rbcf(e, nnn=nn, ll=last):
                    ins = e.matmul(ps[1][:, nnn * NN:(nnn + 1) * NN], p8_sb[:, :],
                                   route_fm[0:8, nnn * NN:(nnn + 1) * NN],
                                   start=True, stop=True)
                    if ll:
                        ins.then_inc(sems["s_rbc"], 1)
                E("tensor", rbcf)
            W("vector", "s_rbc", C["s_rbc"])
            INC("s_hw")
            E("vector", lambda e: e.tensor_tensor(hw_sb[:, :], hstg[:, :],
                                                  ps[1][:, :], ALU.mult)
              .then_inc(sems["s_hw"], 1))

            if debug:
                W("sync", "s_hw", C["s_hw"])
                W('sync', "s_ini", C["s_ini"])
                INC("s_ini", 16)
                E("sync", lambda e, pp=p: e.dma_start(
                    hwdbg[:, pp * TP:(pp + 1) * TP], hw_sb[:, :])
                  .then_inc(sems["s_ini"], 16))

            for h in range(NH):
                nj = JG[h]
                j0 = sum(JG[:h])
                gh = p * NH + h
                def emit_gu(j_abs, q):
                    n = INC("s_gupi")
                    par = (n - 1) % 2
                    if n > 2:
                        W("gpsimd", "s_guc", n - 2)
                    W('gpsimd', f"s_gu{par}", C[f"s_gu{par}"])
                    INC(f"s_gu{par}", 16)
                    E("gpsimd", lambda e, jj=j_abs, qq=q, pr=par:
                      e.dma_start(gub[pr][:, :, :], gu_d[jj, qq])
                      .then_inc(sems[f"s_gu{pr}"], 16))

                def emit_head(hh):
                    W("gpsimd", "s_vb", p * NH + hh)
                    W('gpsimd', "s_uds", C["s_uds"])
                    INC("s_uds", 16)
                    E("gpsimd", lambda e, h_=hh, pp=p: e.dma_start(
                        usl[:, :], u_dd[h_ * 128:(h_ + 1) * 128,
                                        pp * TP:(pp + 1) * TP])
                      .then_inc(sems["s_uds"], 16))
                    W('gpsimd', "s_dds", C["s_dds"])
                    INC("s_dds", 16)
                    E("gpsimd", lambda e, h_=hh, pp=p: e.dma_start(
                        dsl[:, :], dl_dd[h_ * 128:(h_ + 1) * 128,
                                         pp * TP:(pp + 1) * TP])
                      .then_inc(sems["s_dds"], 16))
                    jb = sum(JG[:hh])
                    emit_gu(jb, 0)
                    emit_gu(jb, 1)

                if h == 0:
                    for par in range(2):
                        W("gpsimd", f"s_ud{par}", C[f"s_ud{par}"])
                        W("gpsimd", f"s_dd{par}", C[f"s_dd{par}"])
                    emit_head(0)
                for q in range(2, 4):
                    emit_gu(j0, q)
                for j in range(1, nj):
                    for q in range(4):
                        emit_gu(j0 + j, q)
                if gh == 0:
                    W("tensor", "s_hw", C["s_hw"])
                elif h == 0:
                    W("tensor", "s_hw", C["s_hw"])
                for j in range(nj):
                    jpar = gj % 2
                    psG = ps[0] if jpar == 0 else ps[2]
                    psU = ps[1] if jpar == 0 else ps[3]
                    if gj >= 2:
                        W("tensor", "s_sil", 2 * (gj - 1))
                        W("tensor", "s_pv", 2 * (gj - 1))
                    if j <= 3:
                        W("tensor", "s_oev", C["s_oev"])
                    for q in range(4):
                        pieceno = gj * 4 + q + 1
                        par = (pieceno - 1) % 2
                        W("tensor", f"s_gu{par}", 16 * ((pieceno + 1) // 2))
                        for stype in range(2):
                            pst = psG if stype == 0 else psU
                            co = stype * 128
                            for k in range(8):
                                for nn in range(2):
                                    startf = (q == 0 and k == 0)
                                    stopf = (q == 3 and k == 7)
                                    inc_s = None
                                    if q == 3 and k == 7 and nn == 1 and stype == 1:
                                        inc_s = "s_guc"
                                    elif k == 7 and nn == 1 and stype == 1:
                                        inc_s = "s_guc"
                                    elif stopf and stype == 0 and nn == 0:
                                        inc_s = "s_gG0"
                                    elif stopf and stype == 0 and nn == 1:
                                        inc_s = "s_gG1"
                                    elif stopf and stype == 1 and nn == 0:
                                        inc_s = "s_gU0"
                                    if inc_s:
                                        INC(inc_s)
                                    def p1mm(e, qq=q, kk=k, nnn=nn, sf=startf,
                                             pf=stopf, isem=inc_s, pst_=pst,
                                             pr=par, cc=co):
                                        ins = e.matmul(
                                            pst_[:, nnn * NN:(nnn + 1) * NN],
                                            gub[pr][:, kk, cc:cc + 128],
                                            xn_sb[:, qq * 8 + kk,
                                                  nnn * NN:(nnn + 1) * NN],
                                            start=sf, stop=pf)
                                        if isem:
                                            ins.then_inc(sems[isem], 1)
                                    E("tensor", p1mm)
                    for nn in range(2):
                        W("scalar", f"s_gG{nn}", gj + 1)
                        if gj >= 1:
                            W("scalar", "s_pv", 2 * (gj - 1) + nn + 1)
                        INC("s_sil")
                        E("scalar", lambda e, nnn=nn, pg=psG:
                          e.activation(silg[nnn][:, :],
                                       pg[:, nnn * NN:(nnn + 1) * NN], AF.Silu)
                          .then_inc(sems["s_sil"], 1))
                    for nn in range(2):
                        W("vector", "s_sil", 2 * gj + nn + 1)
                        if nn == 0:
                            W("vector", "s_gU0", gj + 1)
                        else:
                            W("vector", "s_guc", 4 * (gj + 1))
                        INC("s_pv")
                        E("vector", lambda e, nnn=nn, jj=j, pu=psU:
                          e.tensor_tensor(P_sb[:, jj, nnn * NN:(nnn + 1) * NN],
                                          silg[nnn][:, :],
                                          pu[:, nnn * NN:(nnn + 1) * NN],
                                          ALU.mult)
                          .then_inc(sems["s_pv"], 1))
                    gj += 1
                    if j == 0:
                        W("vector", "s_uds", C["s_uds"])
                        W("vector", "s_dds", C["s_dds"])
                        vb = []
                        vb.append(lambda e: e.tensor_scalar(vs[0], dsl[:, :], 2.5,
                                                            None, ALU.is_le))
                        vb.append(lambda e: e.tensor_scalar(vs[1], dsl[:, :], -2.5,
                                                            None, ALU.is_ge))
                        vb.append(lambda e: e.tensor_tensor(vs[0], vs[0],
                                                            vs[1], ALU.mult))
                        vb.append(lambda e: e.tensor_tensor(vs[0], vs[0],
                                                            dsl[:, :], ALU.mult))
                        vb.append(lambda e: e.scalar_tensor_tensor(
                            V_sb[:, 1, :], dsl[:, :], edf_bc[:, 0:1], usl[:, :],
                            ALU.mult, ALU.mult))
                        vb.append(lambda e: e.tensor_tensor(vs[1], dsl[:, :],
                                                            dsl[:, :], ALU.mult))
                        vb.append(lambda e: e.scalar_tensor_tensor(
                            V_sb[:, 2, :], vs[1], edf_bc[:, 1:2], usl[:, :],
                            ALU.mult, ALU.mult))
                        vb.append(lambda e: e.tensor_tensor(vs[1], vs[1],
                                                            dsl[:, :], ALU.mult))
                        vb.append(lambda e: e.scalar_tensor_tensor(
                            V_sb[:, 3, :], vs[1], edf_bc[:, 2:3], usl[:, :],
                            ALU.mult, ALU.mult))
                        vb.append(lambda e: e.tensor_tensor(vs[1], vs[1],
                                                            dsl[:, :], ALU.mult))
                        vb.append(lambda e: e.scalar_tensor_tensor(
                            V_sb[:, 4, :], vs[1], edf_bc[:, 3:4], usl[:, :],
                            ALU.mult, ALU.mult))
                        vb.append(lambda e: e.tensor_tensor(vs[1], vs[1],
                                                            vs[0], ALU.mult))
                        vb.append(lambda e: e.scalar_tensor_tensor(
                            V_sb[:, 5, :], vs[1], edf_bc[:, 4:5], usl[:, :],
                            ALU.mult, ALU.mult))
                        vb.append(lambda e: e.tensor_tensor(vs[1], vs[1],
                                                            dsl[:, :], ALU.mult))
                        vb.append(lambda e: e.scalar_tensor_tensor(
                            V_sb[:, 6, :], vs[1], edf_bc[:, 5:6], usl[:, :],
                            ALU.mult, ALU.mult))
                        for f in vb:
                            VCH(f)
                        INC("s_vb")
                        E("vector", lambda e: e.tensor_copy(V_sb[:, 0, :], usl[:, :])
                          .then_inc(sems["s_vb"], 1))

                rh = RC_H[h]
                hc1 = (rh + 1) // 2
                W("tensor", "s_pv", 2 * gj)
                W("tensor", "s_sil", 2 * gj)
                W("tensor", "s_vb", gh + 1)
                for m in range(32):
                    pout = ps[gm % 4]
                    mp2 = gm % 2
                    for halfw in range(2):
                        nrc = hc1 if halfw == 0 else rh - hc1
                        if gm >= 2:
                            if halfw == 0:
                                W("gpsimd", "s_owc0", gm - 1)
                            else:
                                W("gpsimd", "s_mg", gm - 1)
                        W('gpsimd', f"s_ow{halfw}{mp2}", C[f"s_ow{halfw}{mp2}"])
                        INC(f"s_ow{halfw}{mp2}", 16)
                        rc0 = 0 if halfw == 0 else hc1
                        ofs = OWOFF[h] + m * (128 * rh * 128) + rc0 * 128
                        E("gpsimd", lambda e, of=ofs, nr=nrc, hw_=halfw,
                          pr=mp2, rr=rh:
                          e.dma_start(owb[hw_][pr][:, 0:nr, :],
                                      bass.AP(outw_d.tensor, of,
                                              [[rr * 128, 128], [128, nr],
                                               [1, 128]]))
                          .then_inc(sems[f"s_ow{hw_}{pr}"], 16))
                    if len(pending_acc) >= 2:
                        E("gpsimd", pending_acc.pop(0))
                    if m == 4 and h + 1 < NH:
                        emit_head(h + 1)
                    W("tensor", f"s_ow0{mp2}", C[f"s_ow0{mp2}"])
                    W("tensor", f"s_ow1{mp2}", C[f"s_ow1{mp2}"])
                    if gm >= 4:
                        W("tensor", "s_oev", gm - 3)
                    movings = []
                    for rc in range(rh):
                        if rc < nj:
                            movings.append(("P", rc))
                        elif rc < nj + 7:
                            movings.append(("V", rc - nj))
                        elif rc < nj + 11:
                            movings.append(("X", 4 * h + (rc - nj - 7)))
                        else:
                            movings.append(("HW", 0))
                    for rc in range(rh):
                        hw_i = 0 if rc < hc1 else 1
                        rcl = rc if rc < hc1 else rc - hc1
                        kind, idx = movings[rc]
                        for nn in range(2):
                            startf = (rc == 0)
                            stopf = (rc == rh - 1)
                            inc_s = None
                            if rc == hc1 - 1 and nn == 1:
                                inc_s = "s_owc0"
                            elif stopf and nn == 1:
                                inc_s = "s_mg"
                            if inc_s:
                                INC(inc_s)
                            def p2mm(e, kd=kind, ix=idx, hh=h, nnn=nn, sf=startf,
                                     pf=stopf, isem=inc_s, po=pout, hwi=hw_i,
                                     rl=rcl, mpr=mp2):
                                if kd == "P":
                                    mv = P_sb[:, ix, nnn * NN:(nnn + 1) * NN]
                                elif kd == "V":
                                    mv = V_sb[:, ix, nnn * NN:(nnn + 1) * NN]
                                elif kd == "X":
                                    mv = xn_sb[:, ix, nnn * NN:(nnn + 1) * NN]
                                else:
                                    mv = hw_sb[:, nnn * NN:(nnn + 1) * NN]
                                ins = e.matmul(po[:, nnn * NN:(nnn + 1) * NN],
                                               owb[hwi][mpr][:, rl, :], mv,
                                               start=sf, stop=pf)
                                if isem:
                                    ins.then_inc(sems[isem], 1)
                            E("tensor", p2mm)
                    W("vector", "s_mg", gm + 1)
                    prev2 = stg_last[gm % 2]
                    if prev2 is not None:
                        W("vector", f"s_ac{prev2[0]}", prev2[1])
                    INC("s_oev")
                    if h == 0:
                        E("vector", lambda e, mm=m, po=pout, sl=stg[gm % 2]:
                          e.tensor_scalar(sl[:, :], po[:, :],
                                          b_sb[:, mm:mm + 1], None, ALU.add)
                          .then_inc(sems["s_oev"], 1))
                    else:
                        E("vector", lambda e, po=pout, sl=stg[gm % 2]:
                          e.tensor_copy(sl[:, :], po[:, :])
                          .then_inc(sems["s_oev"], 1))
                    mode = ALU.bypass if h == 0 else ALU.add
                    prevcnt = n_acc[m]
                    n_acc[m] += 16
                    stg_last[gm % 2] = (m, n_acc[m])
                    def accf(e, mm=m, md=mode, oc=gm + 1, pc=prevcnt, pp=p,
                             sl=stg[gm % 2]):
                        e.wait_ge(sems["s_oev"], oc)
                        if pc > 0:
                            e.wait_ge(sems[f"s_ac{mm}"], pc)
                        e.dma_start(acc_dd[mm * 128:(mm + 1) * 128,
                                           pp * TP:(pp + 1) * TP],
                                    sl[:, :], accum_op=md) \
                            .then_inc(sems[f"s_ac{mm}"], 16)
                    pending_acc.append(accf)
                    gm += 1

            for f in pending_acc:
                E("gpsimd", f)
            pending_acc = []

            W("tensor", "s_pv", 2 * gj)
            W("tensor", "s_oev", C["s_oev"])
            for t in range(8):
                r0 = p * TP + t * 128
                for half in range(2):
                    W("sync", "s_xtm", C["s_xtm"])
                    W("sync", "s_4ev", C["s_4ev"])
                    gxl[half] += 1
                    INC(f"s_x{half}", 16)
                    E("sync", lambda e, b=xh[half], rr=r0, hh=half:
                      e.dma_start(b[:, :],
                                  x_d[rr:rr + 128, hh * 2048:(hh + 1) * 2048])
                      .then_inc(sems[f"s_x{hh}"], 16))
                for u2 in range(2):
                    c0 = p * TP + t * 128 + u2 * 64
                    bufs = []
                    for mh in range(2):
                        par = (t * 4 + u2 * 2 + mh) % 2
                        bufs.append(par)
                        W("sync", "s_4tp", C["s_4tp"])
                        for m_ in range(mh * 16, mh * 16 + 16):
                            W("sync", f"s_ac{m_}", n_acc[m_])
                        W('sync', f"s_a4{par}", C[f"s_a4{par}"])
                        INC(f"s_a4{par}", 16)
                        E("sync", lambda e, cc=c0, pr=par, mm=mh:
                          e.dma_start(accb[pr][:, :, :],
                                      acc_dd[mm * 2048:(mm + 1) * 2048, cc:cc + 64]
                                      .rearrange("(mc p) t -> p mc t", p=128))
                          .then_inc(sems[f"s_a4{pr}"], 16))
                    for gg in range(8):
                        slot = gg % 2
                        par = bufs[gg // 4]
                        W("tensor", f"s_a4{par}", C[f"s_a4{par}"])
                        if C["s_4tp"] >= 2:
                            W("tensor", "s_4ev", C["s_4ev"] - 1)
                        for i in range(4):
                            mc = (gg * 4 + i) % 16
                            last = (i == 3)
                            if last:
                                INC("s_4tp")
                            def t4f(e, pr=par, mm=mc, sl=slot, ii=i, ll=last):
                                ins = e.transpose(
                                    ps[1][0:64, sl * 512 + ii * 128:
                                          sl * 512 + (ii + 1) * 128],
                                    accb[pr][:, mm, :], id_sb[:, :])
                                if ll:
                                    ins.then_inc(sems["s_4tp"], 1)
                            E("tensor", t4f)
                        W("vector", "s_4tp", C["s_4tp"])
                        W("vector", "s_x0", 16 * gxl[0])
                        W("vector", "s_x1", 16 * gxl[1])
                        W("vector", "s_out", C["s_out"])
                        INC("s_4ev")
                        E("vector", lambda e, uu=u2, gg_=gg, sl=slot:
                          e.tensor_tensor(
                              out_sb[uu * 64:(uu + 1) * 64,
                                     gg_ * 512:(gg_ + 1) * 512],
                              ps[1][0:64, sl * 512:(sl + 1) * 512],
                              xh[gg_ // 4][uu * 64:(uu + 1) * 64,
                                           (gg_ % 4) * 512:((gg_ % 4) + 1) * 512],
                              ALU.add)
                          .then_inc(sems["s_4ev"], 1))
                W("scalar", "s_4ev", C["s_4ev"])
                W('scalar', "s_out", C["s_out"])
                INC("s_out", 16)
                E("scalar", lambda e, rr=r0:
                  e.dma_start(out_d[rr:rr + 128, :], out_sb[:, :])
                  .then_inc(sems["s_out"], 16))
                if p + 1 < NPASS:
                    if t == 0:
                        emit_s1_guards(p + 1)
                    emit_s1_tile(p + 1, t)
        W("scalar", "s_out", C["s_out"])
        if debug:
            W("sync", "s_ini", C["s_ini"])

        with nc.Block() as blk:
            @blk.tensor
            def _(e):
                for f in prog["tensor"]:
                    f(e)

            @blk.vector
            def _(e):
                for f in prog["vector"]:
                    f(e)

            @blk.scalar
            def _(e):
                for f in prog["scalar"]:
                    f(e)

            @blk.gpsimd
            def _(e):
                for f in prog["gpsimd"]:
                    f(e)

            @blk.sync
            def _(e):
                for f in prog["sync"]:
                    f(e)
    return nc


def _prep_weights(inputs):
    f32 = lambda a: np.ascontiguousarray(np.asarray(a, dtype=np.float32))
    up = f32(inputs["L0_attn_up_w"])
    gate = f32(inputs["L0_attn_gate_w"])
    lp = f32(inputs["L0_attn_local_point"])
    lao = f32(inputs["L0_attn_local_approx_output"])
    fuse = f32(inputs["L0_attn_fuse_weight"])
    df = f32(inputs["L0_attn_discount_factor"])
    fg = f32(inputs["L0_attn_ffn_gate_w"])
    fu = f32(inputs["L0_attn_ffn_up_w"])
    fd = f32(inputs["L0_attn_ffn_down_w"])
    Wb = f32(inputs["L0_ffn_W_base"])
    bb = f32(inputs["L0_ffn_b_base"])
    Wg = f32(inputs["L0_ffn_W_gate"])
    A = f32(inputs["L0_ffn_A"])
    B = f32(inputs["L0_ffn_B"])

    p1cols = np.concatenate([up.T, gate.T, A.reshape(128, H).T], axis=1)
    proj1 = np.ascontiguousarray(
        p1cols.reshape(2, 16, 128, 17, 128).transpose(3, 0, 2, 1, 4))
    wg2 = np.ascontiguousarray(
        Wg.T.reshape(2, 16, 128, 8).transpose(0, 2, 1, 3))
    guc = np.concatenate(
        [np.ascontiguousarray(fg.T).reshape(H, FC, 128),
         np.ascontiguousarray(fu.T).reshape(H, FC, 128)], axis=2)
    gu = np.ascontiguousarray(
        guc.reshape(4, 8, 128, FC, 256).transpose(3, 0, 2, 1, 4))
    downT = np.ascontiguousarray(fd.T)
    laoT = np.ascontiguousarray(lao.T)
    fwT = [np.ascontiguousarray(fuse[:, :, m].T) for m in range(6)]
    WbT = np.ascontiguousarray(Wb.T)
    BT = np.ascontiguousarray(B.transpose(0, 2, 1).reshape(128, H))
    blocks = []
    for h in range(NH):
        j0 = sum(JG[:h])
        blocks.append(downT[j0 * 128:(j0 + JG[h]) * 128])
        blocks.append(laoT[h * 128:(h + 1) * 128])
        for m in range(6):
            blocks.append(fwT[m][h * 128:(h + 1) * 128])
        blocks.append(WbT[h * 512:(h + 1) * 512])
        if h == 0:
            blocks.append(BT)
    outw_rows = np.concatenate(blocks, axis=0)
    assert outw_rows.shape[0] == ROWOFF[-1]
    parts = []
    for h in range(NH):
        rh = RC_H[h]
        blk = outw_rows[ROWOFF[h] // 1:][:rh * 128]
        blk = outw_rows[ROWOFF[h]:ROWOFF[h] + rh * 128]
        b = blk.reshape(rh, 128, 32, 128).transpose(2, 1, 0, 3)
        parts.append(np.ascontiguousarray(b).reshape(-1))
    outw = np.ascontiguousarray(np.concatenate(parts))
    bt = np.ascontiguousarray(bb.reshape(32, 128).T)
    lpt = np.ascontiguousarray(lp.reshape(8, 128).T)
    p8 = np.kron(np.eye(8, dtype=np.float32), np.ones((1, 16), np.float32))
    ident = np.eye(128, dtype=np.float32)
    return dict(proj1=proj1, wg2=wg2, gu=gu, outw=outw, bt=bt, lpt=lpt,
                df=np.ascontiguousarray(df.reshape(1, 6)), p8=p8, ident=ident)


def kernel(**inputs):
    x = np.asarray(inputs["L0_attn_hidden_states"], dtype=np.float32)
    xf = np.ascontiguousarray(x.reshape(NCORE * TCORE, H))
    wmap = _prep_weights(inputs)
    nc = _build_program()
    in_maps = [dict(wmap, x=np.ascontiguousarray(xf[c * TCORE:(c + 1) * TCORE]))
               for c in range(NCORE)]
    res = run_bass_kernel_spmd(nc, in_maps, core_ids=list(range(NCORE)))
    out = np.concatenate([np.asarray(r["out"]) for r in res.results], axis=0)
    return out.reshape(x.shape).astype(np.float32)


if __name__ == "__main__":
    rng = np.random.default_rng(0)
    fake = {
        "L0_attn_hidden_states": rng.standard_normal((2, 8192, 4096), dtype=np.float32),
    }
    print("smoke build only")
    _build_program()
    print("build ok")


# version 14 (final)
# speedup vs baseline: 1.1140x; 1.1140x over previous
import numpy as np
from collections import defaultdict
from contextlib import ExitStack

import concourse.bass as bass
import concourse.mybir as mybir
from concourse.bass_utils import run_bass_kernel_spmd

F32 = mybir.dt.float32
BF16 = mybir.dt.bfloat16
AF = mybir.ActivationFunctionType
ALU = mybir.AluOpType
AX = mybir.AxisListType

NCORE = 8
TCORE = 2048
NPASS = 2
TP = 1024
NN = 512
H, S, FF = 4096, 1024, 9984
HC, FC = 32, 78
JG = [10] * 6 + [9] * 2
NH = 8
RC_H = [JG[h] + 7 + 4 + (1 if h == 0 else 0) for h in range(NH)]
ROWOFF = [0]
for _h in range(NH):
    ROWOFF.append(ROWOFF[-1] + RC_H[_h] * 128)
EPS = 1e-5
ENGS = ("tensor", "vector", "scalar", "gpsimd", "sync")


def _build_program(debug=False):
    nc = bass.Bass("TRN2", target_bir_lowering=False)

    x_d = nc.dram_tensor("x", [TCORE, H], F32, kind="ExternalInput").ap()
    proj1_d = nc.dram_tensor("proj1", [17, 2, 128, 16, 128], F32,
                             kind="ExternalInput").ap()
    wg_d = nc.dram_tensor("wg2", [2, 128, 16, 8], F32, kind="ExternalInput").ap()
    gu_d = nc.dram_tensor("gu", [FC, 4, 128, 8, 256], F32, kind="ExternalInput").ap()
    OW_TOT = sum(32 * 128 * RC_H[h] * 128 for h in range(NH))
    OWOFF = [0]
    for h in range(NH):
        OWOFF.append(OWOFF[-1] + 32 * 128 * RC_H[h] * 128)
    outw_d = nc.dram_tensor("outw", [OW_TOT], F32, kind="ExternalInput").ap()
    bt_d = nc.dram_tensor("bt", [128, 32], F32, kind="ExternalInput").ap()
    lpt_d = nc.dram_tensor("lpt", [128, 8], F32, kind="ExternalInput").ap()
    df_d = nc.dram_tensor("df", [1, 6], F32, kind="ExternalInput").ap()
    p8_d = nc.dram_tensor("p8", [8, 128], F32, kind="ExternalInput").ap()
    id_d = nc.dram_tensor("ident", [128, 128], F32, kind="ExternalInput").ap()
    out_d = nc.dram_tensor("out", [TCORE, H], F32, kind="ExternalOutput").ap()
    if debug:
        u_dd = nc.dram_tensor("u_sc", [S, TCORE], BF16, kind="ExternalOutput").ap()
        dl_dd = nc.dram_tensor("dl_sc", [S, TCORE], F32, kind="ExternalOutput").ap()
        acc_dd = nc.dram_tensor("acc_sc", [H, TCORE], F32, kind="ExternalOutput").ap()
        hwdbg = nc.dram_tensor("hw_dbg", [128, TCORE], BF16, kind="ExternalOutput").ap()
        xndbg = nc.dram_tensor("xn_dbg", [H, TCORE], BF16, kind="ExternalOutput").ap()
        statdbg = nc.dram_tensor("stat_dbg", [128, 128], F32, kind="ExternalOutput").ap()
    else:
        u_dd = nc.dram_tensor("u_sc", [S, TCORE], BF16).ap()
        dl_dd = nc.dram_tensor("dl_sc", [S, TCORE], F32).ap()
        acc_dd = nc.dram_tensor("acc_sc", [H, TCORE], F32).ap()

    st = ExitStack()
    SB = lambda n, shp, dt: st.enter_context(nc.sbuf_tensor(n, shp, dt))
    PS_ = lambda n, shp: st.enter_context(nc.psum_tensor(n, shp, F32))
    sems = {}

    def SEM(n):
        sems[n] = st.enter_context(nc.semaphore(n))

    with st:
        xn_sb = SB("xn", [128, HC, TP], BF16)
        hw_sb = SB("hw", [128, TP], BF16)
        V_sb = SB("V", [128, 7, TP], BF16)
        hstg = SB("hstg", [128, TP], F32)
        edf_bc = SB("edf", [128, 8], F32)
        cpool = SB("cpool", [128, 4], F32)
        ones1 = SB("ones1", [1, 128], F32)
        b_sb = SB("btsb", [128, 32], F32)
        lp_sb = SB("lptsb", [128, 8], F32)
        df_sb = SB("dfsb", [1, 8], F32)
        edf_row = SB("edfrow", [1, 8], F32)
        p8_sb = SB("p8sb", [8, 128], F32)
        id_sb = SB("idsb", [128, 128], F32)
        xh = [SB(f"xh{i}", [128, 2048], F32) for i in range(2)]
        stat = SB("stat", [128, 8], F32)
        p1b = [SB(f"p1b{i}", [128, 16, 128], BF16) for i in range(2)]
        ust0 = SB("ust0", [128, TP], BF16)
        ust = [ust0, ust0]
        lgt_sb = SB("lgt", [8, TP], F32)
        lgt_tm = SB("lgttm", [128, 64], F32)
        w_tm = SB("wtm", [128, 64], F32)
        rt1 = SB("rt1", [128, 64], F32)
        rt2 = SB("rt2", [128, 64], F32)
        rts = SB("rts", [128, 40], F32)
        gub = [SB(f"gub{i}", [128, 8, 256], BF16) for i in range(2)]
        owb = [[SB(f"owb{i}{j}", [128, 11, 128], BF16) for j in range(2)]
               for i in range(2)]
        usl = SB("usl", [128, TP], BF16)
        dsl = SB("dsl", [128, TP], F32)
        vscr = SB("vscr", [128, 2048], F32)
        vs = [bass.AP(vscr, 0, [[2048, 128], [1, TP]]),
              bass.AP(vscr, TP, [[2048, 128], [1, TP]])]
        dst = [vs[0], vs[1]]
        route_fm = lgt_sb
        silg = [SB(f"silg{i}", [128, NN], F32) for i in range(2)]
        P_sb = SB("Psb", [128, 10, TP], BF16)
        stg = [SB(f"stgo{i}", [128, TP], F32) for i in range(2)]
        accb = [SB(f"accb{i}", [128, 16, 64], F32) for i in range(2)]
        out_sb = SB("outsb", [128, H], F32)

        ps = [PS_(f"ps{i}", [128, TP]) for i in range(4)]

        for n in ("s_ini", "s_ims", "s_edf", "s_x0", "s_x1", "s_sqa", "s_var",
                  "s_rstd", "s_xtm", "s_tp", "s_xnf", "s_p10", "s_p11",
                  "s_p1c", "s_s2mm", "s_s2ev", "s_ud0", "s_ud1", "s_dd0",
                  "s_dd1", "s_hev", "s_lev", "s_rtp", "s_r1", "s_r2", "s_r3",
                  "s_r4", "s_r5", "s_rbp", "s_r6", "s_rbc", "s_hw", "s_gu0",
                  "s_gu1", "s_guc", "s_gG0", "s_gG1", "s_gU0", "s_sil",
                  "s_pv", "s_vb", "s_uds", "s_dds", "s_ow00", "s_ow01", "s_ow10", "s_ow11",
                  "s_owc0", "s_mg", "s_oev", "s_a40", "s_a41", "s_4tp",
                  "s_4ev", "s_out", "s_vch", "s_sch", "s_xnfA", "s_xnfD"):
            SEM(n)
        for m in range(32):
            SEM(f"s_ac{m}")

        C = defaultdict(int)
        prog = {e: [] for e in ENGS}

        def E(eng, f):
            prog[eng].append(f)

        def W(eng, sem, val):
            if val > 0:
                E(eng, lambda e, s=sems[sem], v=val: e.wait_ge(s, v))

        def INC(sem, n=1):
            C[sem] += n
            return C[sem]

        def VCH(fn):
            v = INC("s_vch")
            E("vector", lambda e, f=fn: f(e).then_inc(sems["s_vch"], 1))
            W("vector", "s_vch", v)

        def SCH(fn):
            v = INC("s_sch")
            E("scalar", lambda e, f=fn: f(e).then_inc(sems["s_sch"], 1))
            W("scalar", "s_sch", v)

        for dst_, src_ in ((b_sb[:, :], bt_d), (lp_sb[:, :], lpt_d),
                           (df_sb[0:1, 0:6], df_d), (p8_sb[:, :], p8_d),
                           (id_sb[:, :], id_d)):
            W('sync', "s_ini", C["s_ini"])
            INC("s_ini", 16)
            E("sync", lambda e, d=dst_, s=src_: e.dma_start(d, s)
              .then_inc(sems["s_ini"], 16))
        for col, val in ((0, EPS), (1, -0.5), (2, -1.0)):
            E("gpsimd", lambda e, c=col, v=val: e.memset(cpool[:, c:c + 1], v))
        E("gpsimd", lambda e: e.memset(ones1[:, :], 1.0))
        INC("s_ims")
        E("gpsimd", lambda e: e.engine_nop().then_inc(sems["s_ims"], 1))
        W("scalar", "s_ini", C["s_ini"])
        W("scalar", "s_ims", C["s_ims"])
        INC("s_edf")
        E("scalar", lambda e: e.activation(edf_row[0:1, 0:6], df_sb[0:1, 0:6],
                                           AF.Exp, scale=cpool[0:1, 2:3])
          .then_inc(sems["s_edf"], 1))
        W("tensor", "s_edf", C["s_edf"])
        INC("s_edf")
        E("tensor", lambda e: e.matmul(ps[0][:, 0:6], ones1[0:1, :],
                                       edf_row[0:1, 0:6], start=True, stop=True)
          .then_inc(sems["s_edf"], 1))
        W("vector", "s_edf", C["s_edf"])
        INC("s_edf")
        E("vector", lambda e: e.tensor_copy(edf_bc[:, 0:6], ps[0][:, 0:6])
          .then_inc(sems["s_edf"], 1))

        gj = 0
        gm = 0
        gpiece = 0
        gtile = 0
        ghalf = 0
        gxl = [0, 0]
        gtp = 0
        deferred_acc = []
        n_acc = [0] * 32
        stg_last = [None, None]
        pending_acc = []
        stg_used = [(0, 0), (0, 0)]
        stg_xtm = [0, 0]
        tpq = [0, 0]

        for p in range(NPASS):
            def emit_s1_guards(pp):
                if pp > 0:
                    W("scalar", "s_mg", gm)
                    W("vector", "s_mg", gm)
                    W("scalar", "s_vb", NH * pp)
                    W("vector", "s_vb", NH * pp)
                    W("sync", "s_ac30", n_acc[30])
                    W("sync", "s_ac31", n_acc[31])
                    W("sync", "s_vb", NH * pp)
                    W("tensor", "s_oev", C["s_oev"])

            def emit_s1_tile(pp, t):
                nonlocal ghalf, gtp, gtile
                r0 = pp * TP + t * 128
                for q in range(4):
                    uv, us = stg_used[q % 2]
                    W("sync", "s_vch", uv)
                    W("sync", "s_sqa", us)
                    W('sync', f"s_x{q % 2}", C[f"s_x{q % 2}"])
                    INC(f"s_x{q % 2}", 16)
                    E("sync", lambda e, rr=r0, qq=q:
                      e.dma_start(stg[qq % 2][:, :],
                                  x_d[rr:rr + 128, qq * TP:(qq + 1) * TP])
                      .then_inc(sems[f"s_x{qq % 2}"], 16))
                    W("vector", f"s_x{q % 2}", C[f"s_x{q % 2}"])
                    if q == 0:
                        VCH(lambda e, qq=q: e.reduce_sum(
                            stat[:, 0:1], stg[qq % 2][:, :], axis=AX.X))
                    else:
                        VCH(lambda e, qq=q: e.reduce_sum(
                            stat[:, 1:2], stg[qq % 2][:, :], axis=AX.X))
                        VCH(lambda e: e.tensor_add(stat[:, 0:1], stat[:, 0:1],
                                                   stat[:, 1:2]))
                    W("scalar", f"s_x{q % 2}", C[f"s_x{q % 2}"])
                    W("scalar", "s_vch", C["s_vch"])
                    INC("s_sqa")
                    E("scalar", lambda e, qq=q:
                      e.activation(stg[qq % 2][:, :], stg[qq % 2][:, :], AF.Square,
                                   accum_out=stat[:, 2:3] if qq == 0
                                   else stat[:, 3:4])
                      .then_inc(sems["s_sqa"], 1))
                    if q > 0:
                        W("vector", "s_sqa", C["s_sqa"])
                        VCH(lambda e: e.tensor_add(stat[:, 2:3], stat[:, 2:3],
                                                   stat[:, 3:4]))
                    stg_used[q % 2] = (C["s_vch"], C["s_sqa"])
                VCH(lambda e: e.tensor_scalar_mul(stat[:, 4:5], stat[:, 0:1], 1.0 / H))
                VCH(lambda e: e.tensor_mul(stat[:, 7:8], stat[:, 4:5], stat[:, 4:5]))
                INC("s_var")
                E("vector", lambda e: e.scalar_tensor_tensor(
                    stat[:, 5:6], stat[:, 2:3], 1.0 / H, stat[:, 7:8],
                    ALU.mult, ALU.subtract).then_inc(sems["s_var"], 1))
                W("scalar", "s_var", C["s_var"])
                SCH(lambda e: e.activation(stat[:, 6:7], stat[:, 5:6],
                                           AF.Ln, bias=cpool[:, 0:1]))
                INC("s_rstd")
                E("scalar", lambda e: e.activation(stat[:, 6:7], stat[:, 6:7],
                                                   AF.Exp, scale=cpool[:, 1:2])
                  .then_inc(sems["s_rstd"], 1))
                if debug:
                    W("sync", "s_rstd", C["s_rstd"])
                    W('sync', "s_ini", C["s_ini"])
                    INC("s_ini", 16)
                    E("sync", lambda e, pt=p * 8 + t: e.dma_start(
                        statdbg[:, pt * 8:(pt + 1) * 8], stat[:, :])
                      .then_inc(sems["s_ini"], 16))
                for q in range(4):
                    vslot = q % 2
                    W("sync", "s_tp", tpq[vslot])
                    W('sync', f"s_x{q % 2}", C[f"s_x{q % 2}"])
                    INC(f"s_x{q % 2}", 16)
                    E("sync", lambda e, rr=r0, qq=q, vs_=vslot:
                      e.dma_start(vscr[:, vs_ * TP:(vs_ + 1) * TP],
                                  x_d[rr:rr + 128, qq * TP:(qq + 1) * TP])
                      .then_inc(sems[f"s_x{qq % 2}"], 16))
                    W("vector", f"s_x{q % 2}", C[f"s_x{q % 2}"])
                    W("vector", "s_rstd", C["s_rstd"])
                    INC("s_xtm")
                    E("vector", lambda e, vs_=vslot:
                      e.tensor_scalar(vscr[:, vs_ * TP:(vs_ + 1) * TP],
                                      vscr[:, vs_ * TP:(vs_ + 1) * TP],
                                      stat[:, 4:5], stat[:, 6:7],
                                      ALU.subtract, ALU.mult)
                      .then_inc(sems["s_xtm"], 1))
                    for g in range(2):
                        slot = (q * 2 + g) % 2
                        pslot = ps[0][:, slot * 512:slot * 512 + 512]
                        W("tensor", "s_xtm", C["s_xtm"])
                        sl_sem = "s_xnfA" if slot == 0 else "s_xnfD"
                        if C[sl_sem] > 0:
                            W("tensor", sl_sem, C[sl_sem])
                        else:
                            W("tensor", "s_edf", 3)
                        for i in range(4):
                            last = (i == 3)
                            if last:
                                INC("s_tp")
                            def tpf(e, vs_=vslot, gg=g, ii=i, ll=last, psl=pslot):
                                ins = e.transpose(
                                    psl[:, ii * 128:(ii + 1) * 128],
                                    vscr[:, vs_ * TP + (gg * 4 + ii) * 128:
                                         vs_ * TP + (gg * 4 + ii + 1) * 128],
                                    id_sb[:, :])
                                if ll:
                                    ins.then_inc(sems["s_tp"], 1)
                            E("tensor", tpf)
                        gtp = C["s_tp"]
                        kb = q * 8 + g * 4
                        eng = "scalar" if slot == 0 else "vector"
                        sem_e = "s_xnfA" if slot == 0 else "s_xnfD"
                        W(eng, "s_tp", gtp)
                        INC(sem_e)
                        if eng == "scalar":
                            E(eng, lambda e, kb_=kb, tt=t, so=slot, se=sem_e:
                              e.copy(xn_sb[:, kb_:kb_ + 4, tt * 128:(tt + 1) * 128],
                                     bass.AP(ps[0], so * 512,
                                             [[1024, 128], [128, 4], [1, 128]]))
                              .then_inc(sems[se], 1))
                        else:
                            E(eng, lambda e, kb_=kb, tt=t, so=slot, se=sem_e:
                              e.tensor_copy(
                                  xn_sb[:, kb_:kb_ + 4, tt * 128:(tt + 1) * 128],
                                  bass.AP(ps[0], so * 512,
                                          [[1024, 128], [128, 4], [1, 128]]))
                              .then_inc(sems[se], 1))
                    tpq[vslot] = C["s_tp"]
                gtile += 1

            if p == 0:
                emit_s1_guards(0)
                for t in range(8):
                    emit_s1_tile(0, t)

            if debug:
                W("sync", "s_xnfA", C["s_xnfA"])
                W("sync", "s_xnfD", C["s_xnfD"])
                W('sync', "s_ini", C["s_ini"])
                INC("s_ini", 16)
                E("sync", lambda e, pp=p: e.dma_start(
                    xndbg[:, pp * TP:(pp + 1) * TP]
                    .rearrange("(kc p) t -> p kc t", p=128),
                    xn_sb[:, :, :]).then_inc(sems["s_ini"], 16))

            colw = [128] * 17 + [8]
            coff = [c * 128 for c in range(17)] + [2176]
            for c in range(18):
                for q in range(2):
                    if p * 18 + c >= 1:
                        if q == 0:
                            W("gpsimd", "s_p1c", p * 18 + c)
                        else:
                            W("gpsimd", "s_s2mm", p * 18 + c)
                    W('gpsimd', f"s_p1{q}", C[f"s_p1{q}"])
                    INC(f"s_p1{q}", 16)
                    if c < 17:
                        E("gpsimd", lambda e, qq=q, cc=c:
                          e.dma_start(p1b[qq][:, :, 0:128], proj1_d[cc, qq])
                          .then_inc(sems[f"s_p1{qq}"], 16))
                    else:
                        E("gpsimd", lambda e, qq=q:
                          e.dma_start(p1b[qq][:, :, 0:8], wg_d[qq])
                          .then_inc(sems[f"s_p1{qq}"], 16))
            for c in range(18):
                cw = colw[c]
                slot = ps[1] if c % 2 == 0 else ps[2]
                if p * 18 + c >= 2:
                    W("tensor", "s_s2ev", p * 18 + c - 1)
                elif p == 0 and c == 0:
                    W("tensor", "s_xnfA", C["s_xnfA"])
                    W("tensor", "s_xnfD", C["s_xnfD"])
                if p > 0 and c == 0:
                    W("tensor", "s_4ev", C["s_4ev"])
                    W("tensor", "s_xnfA", C["s_xnfA"])
                    W("tensor", "s_xnfD", C["s_xnfD"])
                for q in range(2):
                    W("tensor", f"s_p1{q}", 16 * (p * 18 + c + 1))
                    for k in range(16):
                        for nn in range(2):
                            startf = (q == 0 and k == 0)
                            stopf = (q == 1 and k == 15)
                            i0 = (q == 0 and k == 15 and nn == 1)
                            i1 = (stopf and nn == 1)
                            if i0:
                                INC("s_p1c")
                            if i1:
                                INC("s_s2mm")
                            def s2mm(e, qq=q, kk=k, nnn=nn, sf=startf, pf=stopf,
                                     l0=i0, l1=i1, sl=slot, ww=cw):
                                o = sl[0:ww, nnn * NN:nnn * NN + NN] if ww < 128 \
                                    else sl[:, nnn * NN:nnn * NN + NN]
                                ins = e.matmul(o, p1b[qq][:, kk, 0:ww],
                                               xn_sb[:, qq * 16 + kk,
                                                     nnn * NN:(nnn + 1) * NN],
                                               start=sf, stop=pf)
                                if l0:
                                    ins.then_inc(sems["s_p1c"], 1)
                                elif l1:
                                    ins.then_inc(sems["s_s2mm"], 1)
                            E("tensor", s2mm)
                W("vector", "s_s2mm", C["s_s2mm"])
                if c < 8:
                    par = c % 2
                    if p * 8 + c >= 1:
                        W("vector", "s_ud0", C["s_ud0"])
                        W("vector", "s_ud1", C["s_ud1"])
                    INC("s_s2ev")
                    E("vector", lambda e, pr=par, sl=slot:
                      e.tensor_copy(ust[pr][:, :], sl[:, :])
                      .then_inc(sems["s_s2ev"], 1))
                    W("scalar", "s_s2ev", C["s_s2ev"])
                    W('scalar', f"s_ud{par}", C[f"s_ud{par}"])
                    INC(f"s_ud{par}", 16)
                    E("scalar", lambda e, cc=c, pr=par, pp=p:
                      e.dma_start(u_dd[cc * 128:(cc + 1) * 128,
                                       pp * TP:(pp + 1) * TP], ust[pr][:, :])
                      .then_inc(sems[f"s_ud{pr}"], 16))
                elif c < 16:
                    sc = c - 8
                    par = sc % 2
                    if p * 8 + sc >= 2:
                        W("vector", f"s_dd{par}", C[f"s_dd{par}"])
                    INC("s_s2ev")
                    E("vector", lambda e, pr=par, sl=slot, ss=sc:
                      e.tensor_scalar(dst[pr][:, :], sl[:, :],
                                      lp_sb[:, ss:ss + 1], None, ALU.subtract)
                      .then_inc(sems["s_s2ev"], 1))
                    W("scalar", "s_s2ev", C["s_s2ev"])
                    W('scalar', f"s_dd{par}", C[f"s_dd{par}"])
                    INC(f"s_dd{par}", 16)
                    E("scalar", lambda e, ss=sc, pr=par, pp=p:
                      e.dma_start(dl_dd[ss * 128:(ss + 1) * 128,
                                        pp * TP:(pp + 1) * TP], dst[pr][:, :])
                      .then_inc(sems[f"s_dd{pr}"], 16))
                elif c == 16:
                    INC("s_s2ev")
                    INC("s_hev")
                    E("vector", lambda e, sl=slot:
                      e.tensor_copy(hstg[:, :], sl[:, :])
                      .then_inc(sems["s_hev"], 1))
                    E("vector", lambda e:
                      e.engine_nop().then_inc(sems["s_s2ev"], 1))
                else:
                    INC("s_s2ev")
                    INC("s_lev")
                    E("vector", lambda e, sl=slot:
                      e.tensor_copy(lgt_sb[0:8, :], sl[0:8, :])
                      .then_inc(sems["s_lev"], 1))
                    E("vector", lambda e:
                      e.engine_nop().then_inc(sems["s_s2ev"], 1))
            W("tensor", "s_lev", C["s_lev"])
            W("tensor", "s_hev", C["s_hev"])
            for q in range(8):
                last = (q == 7)
                if last:
                    INC("s_rtp")
                def rtf(e, qq=q, ll=last):
                    ins = e.transpose(ps[1][0:128, qq * 8:(qq + 1) * 8],
                                      lgt_sb[0:8, qq * 128:(qq + 1) * 128],
                                      id_sb[0:8, 0:8])
                    if ll:
                        ins.then_inc(sems["s_rtp"], 1)
                E("tensor", rtf)
            lg3 = bass.AP(lgt_tm, 0, [[64, 128], [8, 8], [1, 8]])
            m1b = bass.AP(rts, 0, [[40, 128], [1, 8], [0, 8]])
            m2b = bass.AP(rts, 8, [[40, 128], [1, 8], [0, 8]])
            rdb = bass.AP(rts, 32, [[40, 128], [1, 8], [0, 8]])
            r13 = bass.AP(rt1, 0, [[64, 128], [8, 8], [1, 8]])
            r23 = bass.AP(rt2, 0, [[64, 128], [8, 8], [1, 8]])
            w3 = bass.AP(w_tm, 0, [[64, 128], [8, 8], [1, 8]])
            m1o = bass.AP(rts, 0, [[40, 128], [1, 8]])
            m2o = bass.AP(rts, 8, [[40, 128], [1, 8]])
            W("vector", "s_rtp", C["s_rtp"])
            VCH(lambda e: e.tensor_copy(lgt_tm[:, :], ps[1][:, 0:64]))
            VCH(lambda e: e.reduce_max(m1o, lg3, axis=AX.X))
            VCH(lambda e: e.tensor_tensor(r13, lg3, m1b, ALU.is_equal))
            VCH(lambda e: e.scalar_tensor_tensor(r23, r13, -1e30, lg3,
                                                 ALU.mult, ALU.add))
            VCH(lambda e: e.reduce_max(m2o, r23, axis=AX.X))
            INC("s_r1")
            E("vector", lambda e: e.tensor_sub(rts[:, 16:24], rts[:, 8:16],
                                               rts[:, 0:8])
              .then_inc(sems["s_r1"], 1))
            W("scalar", "s_r1", C["s_r1"])
            INC("s_r2")
            E("scalar", lambda e: e.activation(rts[:, 16:24], rts[:, 16:24],
                                               AF.Exp).then_inc(sems["s_r2"], 1))
            W("vector", "s_r2", C["s_r2"])
            VCH(lambda e: e.tensor_scalar_add(rts[:, 24:32],
                                              rts[:, 16:24], 1.0))
            VCH(lambda e: e.reciprocal(rts[:, 32:40], rts[:, 24:32]))
            VCH(lambda e: e.tensor_tensor(r23, lg3, m2b, ALU.is_ge))
            INC("s_r3")
            E("vector", lambda e: e.tensor_tensor(r13, lg3, m1b, ALU.subtract)
              .then_inc(sems["s_r3"], 1))
            W("scalar", "s_r3", C["s_r3"])
            INC("s_r4")
            E("scalar", lambda e: e.activation(rt1[:, :], rt1[:, :], AF.Exp)
              .then_inc(sems["s_r4"], 1))
            W("vector", "s_r4", C["s_r4"])
            VCH(lambda e: e.tensor_tensor(w3, r13, r23, ALU.mult))
            INC("s_r5")
            E("vector", lambda e: e.tensor_tensor(w3, w3, rdb, ALU.mult)
              .then_inc(sems["s_r5"], 1))
            W("tensor", "s_r5", C["s_r5"])
            for q in range(8):
                last = (q == 7)
                if last:
                    INC("s_rbp")
                def rbf(e, qq=q, ll=last):
                    ins = e.transpose(ps[3][0:8, qq * 128:(qq + 1) * 128],
                                      w_tm[:, qq * 8:(qq + 1) * 8], id_sb[:, :])
                    if ll:
                        ins.then_inc(sems["s_rbp"], 1)
                E("tensor", rbf)
            W("vector", "s_rbp", C["s_rbp"])
            INC("s_r6")
            E("vector", lambda e: e.tensor_copy(route_fm[0:8, :], ps[3][0:8, :])
              .then_inc(sems["s_r6"], 1))
            W("tensor", "s_r6", C["s_r6"])
            for nn in range(2):
                last = (nn == 1)
                if last:
                    INC("s_rbc")
                def rbcf(e, nnn=nn, ll=last):
                    ins = e.matmul(ps[1][:, nnn * NN:(nnn + 1) * NN], p8_sb[:, :],
                                   route_fm[0:8, nnn * NN:(nnn + 1) * NN],
                                   start=True, stop=True)
                    if ll:
                        ins.then_inc(sems["s_rbc"], 1)
                E("tensor", rbcf)
            W("vector", "s_rbc", C["s_rbc"])
            INC("s_hw")
            E("vector", lambda e: e.tensor_tensor(hw_sb[:, :], hstg[:, :],
                                                  ps[1][:, :], ALU.mult)
              .then_inc(sems["s_hw"], 1))

            if debug:
                W("sync", "s_hw", C["s_hw"])
                W('sync', "s_ini", C["s_ini"])
                INC("s_ini", 16)
                E("sync", lambda e, pp=p: e.dma_start(
                    hwdbg[:, pp * TP:(pp + 1) * TP], hw_sb[:, :])
                  .then_inc(sems["s_ini"], 16))

            for h in range(NH):
                nj = JG[h]
                j0 = sum(JG[:h])
                gh = p * NH + h
                def emit_gu(j_abs, q):
                    n = INC("s_gupi")
                    par = (n - 1) % 2
                    if n > 2:
                        W("gpsimd", "s_guc", n - 2)
                    W('gpsimd', f"s_gu{par}", C[f"s_gu{par}"])
                    INC(f"s_gu{par}", 16)
                    E("gpsimd", lambda e, jj=j_abs, qq=q, pr=par:
                      e.dma_start(gub[pr][:, :, :], gu_d[jj, qq])
                      .then_inc(sems[f"s_gu{pr}"], 16))

                def emit_head(hh):
                    W("gpsimd", "s_vb", p * NH + hh)
                    W('gpsimd', "s_uds", C["s_uds"])
                    INC("s_uds", 16)
                    E("gpsimd", lambda e, h_=hh, pp=p: e.dma_start(
                        usl[:, :], u_dd[h_ * 128:(h_ + 1) * 128,
                                        pp * TP:(pp + 1) * TP])
                      .then_inc(sems["s_uds"], 16))
                    W('gpsimd', "s_dds", C["s_dds"])
                    INC("s_dds", 16)
                    E("gpsimd", lambda e, h_=hh, pp=p: e.dma_start(
                        dsl[:, :], dl_dd[h_ * 128:(h_ + 1) * 128,
                                         pp * TP:(pp + 1) * TP])
                      .then_inc(sems["s_dds"], 16))
                    jb = sum(JG[:hh])
                    emit_gu(jb, 0)
                    emit_gu(jb, 1)

                if h == 0:
                    for par in range(2):
                        W("gpsimd", f"s_ud{par}", C[f"s_ud{par}"])
                        W("gpsimd", f"s_dd{par}", C[f"s_dd{par}"])
                    emit_head(0)
                for q in range(2, 4):
                    emit_gu(j0, q)
                for j in range(1, nj):
                    for q in range(4):
                        emit_gu(j0 + j, q)
                if gh == 0:
                    W("tensor", "s_hw", C["s_hw"])
                elif h == 0:
                    W("tensor", "s_hw", C["s_hw"])
                for j in range(nj):
                    jpar = gj % 2
                    psG = ps[0] if jpar == 0 else ps[2]
                    psU = ps[1] if jpar == 0 else ps[3]
                    if gj >= 2:
                        W("tensor", "s_sil", 2 * (gj - 1))
                        W("tensor", "s_pv", 2 * (gj - 1))
                    if j <= 3:
                        W("tensor", "s_oev", C["s_oev"])
                    for q in range(4):
                        pieceno = gj * 4 + q + 1
                        par = (pieceno - 1) % 2
                        W("tensor", f"s_gu{par}", 16 * ((pieceno + 1) // 2))
                        for stype in range(2):
                            pst = psG if stype == 0 else psU
                            co = stype * 128
                            for k in range(8):
                                for nn in range(2):
                                    startf = (q == 0 and k == 0)
                                    stopf = (q == 3 and k == 7)
                                    inc_s = None
                                    if q == 3 and k == 7 and nn == 1 and stype == 1:
                                        inc_s = "s_guc"
                                    elif k == 7 and nn == 1 and stype == 1:
                                        inc_s = "s_guc"
                                    elif stopf and stype == 0 and nn == 0:
                                        inc_s = "s_gG0"
                                    elif stopf and stype == 0 and nn == 1:
                                        inc_s = "s_gG1"
                                    elif stopf and stype == 1 and nn == 0:
                                        inc_s = "s_gU0"
                                    if inc_s:
                                        INC(inc_s)
                                    def p1mm(e, qq=q, kk=k, nnn=nn, sf=startf,
                                             pf=stopf, isem=inc_s, pst_=pst,
                                             pr=par, cc=co):
                                        ins = e.matmul(
                                            pst_[:, nnn * NN:(nnn + 1) * NN],
                                            gub[pr][:, kk, cc:cc + 128],
                                            xn_sb[:, qq * 8 + kk,
                                                  nnn * NN:(nnn + 1) * NN],
                                            start=sf, stop=pf)
                                        if isem:
                                            ins.then_inc(sems[isem], 1)
                                    E("tensor", p1mm)
                    for nn in range(2):
                        W("scalar", f"s_gG{nn}", gj + 1)
                        if gj >= 1:
                            W("scalar", "s_pv", 2 * (gj - 1) + nn + 1)
                        INC("s_sil")
                        E("scalar", lambda e, nnn=nn, pg=psG:
                          e.activation(silg[nnn][:, :],
                                       pg[:, nnn * NN:(nnn + 1) * NN], AF.Silu)
                          .then_inc(sems["s_sil"], 1))
                    for nn in range(2):
                        W("vector", "s_sil", 2 * gj + nn + 1)
                        if nn == 0:
                            W("vector", "s_gU0", gj + 1)
                        else:
                            W("vector", "s_guc", 4 * (gj + 1))
                        INC("s_pv")
                        E("vector", lambda e, nnn=nn, jj=j, pu=psU:
                          e.tensor_tensor(P_sb[:, jj, nnn * NN:(nnn + 1) * NN],
                                          silg[nnn][:, :],
                                          pu[:, nnn * NN:(nnn + 1) * NN],
                                          ALU.mult)
                          .then_inc(sems["s_pv"], 1))
                    gj += 1
                    if j == 0:
                        W("vector", "s_uds", C["s_uds"])
                        W("vector", "s_dds", C["s_dds"])
                        vb = []
                        vb.append(lambda e: e.tensor_scalar(vs[0], dsl[:, :], 2.5,
                                                            None, ALU.is_le))
                        vb.append(lambda e: e.tensor_scalar(vs[1], dsl[:, :], -2.5,
                                                            None, ALU.is_ge))
                        vb.append(lambda e: e.tensor_tensor(vs[0], vs[0],
                                                            vs[1], ALU.mult))
                        vb.append(lambda e: e.tensor_tensor(vs[0], vs[0],
                                                            dsl[:, :], ALU.mult))
                        vb.append(lambda e: e.scalar_tensor_tensor(
                            V_sb[:, 1, :], dsl[:, :], edf_bc[:, 0:1], usl[:, :],
                            ALU.mult, ALU.mult))
                        vb.append(lambda e: e.tensor_tensor(vs[1], dsl[:, :],
                                                            dsl[:, :], ALU.mult))
                        vb.append(lambda e: e.scalar_tensor_tensor(
                            V_sb[:, 2, :], vs[1], edf_bc[:, 1:2], usl[:, :],
                            ALU.mult, ALU.mult))
                        vb.append(lambda e: e.tensor_tensor(vs[1], vs[1],
                                                            dsl[:, :], ALU.mult))
                        vb.append(lambda e: e.scalar_tensor_tensor(
                            V_sb[:, 3, :], vs[1], edf_bc[:, 2:3], usl[:, :],
                            ALU.mult, ALU.mult))
                        vb.append(lambda e: e.tensor_tensor(vs[1], vs[1],
                                                            dsl[:, :], ALU.mult))
                        vb.append(lambda e: e.scalar_tensor_tensor(
                            V_sb[:, 4, :], vs[1], edf_bc[:, 3:4], usl[:, :],
                            ALU.mult, ALU.mult))
                        vb.append(lambda e: e.tensor_tensor(vs[1], vs[1],
                                                            vs[0], ALU.mult))
                        vb.append(lambda e: e.scalar_tensor_tensor(
                            V_sb[:, 5, :], vs[1], edf_bc[:, 4:5], usl[:, :],
                            ALU.mult, ALU.mult))
                        vb.append(lambda e: e.tensor_tensor(vs[1], vs[1],
                                                            dsl[:, :], ALU.mult))
                        vb.append(lambda e: e.scalar_tensor_tensor(
                            V_sb[:, 6, :], vs[1], edf_bc[:, 5:6], usl[:, :],
                            ALU.mult, ALU.mult))
                        for f in vb:
                            VCH(f)
                        INC("s_vb")
                        E("vector", lambda e: e.tensor_copy(V_sb[:, 0, :], usl[:, :])
                          .then_inc(sems["s_vb"], 1))

                rh = RC_H[h]
                hc1 = (rh + 1) // 2
                W("tensor", "s_pv", 2 * gj)
                W("tensor", "s_sil", 2 * gj)
                W("tensor", "s_vb", gh + 1)
                for m in range(32):
                    pout = ps[gm % 4]
                    mp2 = gm % 2
                    for halfw in range(2):
                        nrc = hc1 if halfw == 0 else rh - hc1
                        if gm >= 2:
                            if halfw == 0:
                                W("gpsimd", "s_owc0", gm - 1)
                            else:
                                W("gpsimd", "s_mg", gm - 1)
                        W('gpsimd', f"s_ow{halfw}{mp2}", C[f"s_ow{halfw}{mp2}"])
                        INC(f"s_ow{halfw}{mp2}", 16)
                        rc0 = 0 if halfw == 0 else hc1
                        ofs = OWOFF[h] + m * (128 * rh * 128) + rc0 * 128
                        E("gpsimd", lambda e, of=ofs, nr=nrc, hw_=halfw,
                          pr=mp2, rr=rh:
                          e.dma_start(owb[hw_][pr][:, 0:nr, :],
                                      bass.AP(outw_d.tensor, of,
                                              [[rr * 128, 128], [128, nr],
                                               [1, 128]]))
                          .then_inc(sems[f"s_ow{hw_}{pr}"], 16))
                    if len(pending_acc) >= 2:
                        E("gpsimd", pending_acc.pop(0))
                    if m == 4 and h + 1 < NH:
                        emit_head(h + 1)
                    W("tensor", f"s_ow0{mp2}", C[f"s_ow0{mp2}"])
                    W("tensor", f"s_ow1{mp2}", C[f"s_ow1{mp2}"])
                    if gm >= 4:
                        W("tensor", "s_oev", gm - 3)
                    movings = []
                    for rc in range(rh):
                        if rc < nj:
                            movings.append(("P", rc))
                        elif rc < nj + 7:
                            movings.append(("V", rc - nj))
                        elif rc < nj + 11:
                            movings.append(("X", 4 * h + (rc - nj - 7)))
                        else:
                            movings.append(("HW", 0))
                    for rc in range(rh):
                        hw_i = 0 if rc < hc1 else 1
                        rcl = rc if rc < hc1 else rc - hc1
                        kind, idx = movings[rc]
                        for nn in range(2):
                            startf = (rc == 0)
                            stopf = (rc == rh - 1)
                            inc_s = None
                            if rc == hc1 - 1 and nn == 1:
                                inc_s = "s_owc0"
                            elif stopf and nn == 1:
                                inc_s = "s_mg"
                            if inc_s:
                                INC(inc_s)
                            def p2mm(e, kd=kind, ix=idx, hh=h, nnn=nn, sf=startf,
                                     pf=stopf, isem=inc_s, po=pout, hwi=hw_i,
                                     rl=rcl, mpr=mp2):
                                if kd == "P":
                                    mv = P_sb[:, ix, nnn * NN:(nnn + 1) * NN]
                                elif kd == "V":
                                    mv = V_sb[:, ix, nnn * NN:(nnn + 1) * NN]
                                elif kd == "X":
                                    mv = xn_sb[:, ix, nnn * NN:(nnn + 1) * NN]
                                else:
                                    mv = hw_sb[:, nnn * NN:(nnn + 1) * NN]
                                ins = e.matmul(po[:, nnn * NN:(nnn + 1) * NN],
                                               owb[hwi][mpr][:, rl, :], mv,
                                               start=sf, stop=pf)
                                if isem:
                                    ins.then_inc(sems[isem], 1)
                            E("tensor", p2mm)
                    W("vector", "s_mg", gm + 1)
                    prev2 = stg_last[gm % 2]
                    if prev2 is not None:
                        W("vector", f"s_ac{prev2[0]}", prev2[1])
                    INC("s_oev")
                    if h == 0:
                        E("vector", lambda e, mm=m, po=pout, sl=stg[gm % 2]:
                          e.tensor_scalar(sl[:, :], po[:, :],
                                          b_sb[:, mm:mm + 1], None, ALU.add)
                          .then_inc(sems["s_oev"], 1))
                    else:
                        E("vector", lambda e, po=pout, sl=stg[gm % 2]:
                          e.tensor_copy(sl[:, :], po[:, :])
                          .then_inc(sems["s_oev"], 1))
                    mode = ALU.bypass if h == 0 else ALU.add
                    prevcnt = n_acc[m]
                    n_acc[m] += 16
                    stg_last[gm % 2] = (m, n_acc[m])
                    def accf(e, mm=m, md=mode, oc=gm + 1, pc=prevcnt, pp=p,
                             sl=stg[gm % 2]):
                        e.wait_ge(sems["s_oev"], oc)
                        if pc > 0:
                            e.wait_ge(sems[f"s_ac{mm}"], pc)
                        e.dma_start(acc_dd[mm * 128:(mm + 1) * 128,
                                           pp * TP:(pp + 1) * TP],
                                    sl[:, :], accum_op=md) \
                            .then_inc(sems[f"s_ac{mm}"], 16)
                    pending_acc.append(accf)
                    gm += 1

            for f in pending_acc:
                E("gpsimd", f)
            pending_acc = []

            W("tensor", "s_pv", 2 * gj)
            W("tensor", "s_oev", C["s_oev"])
            for t in range(8):
                r0 = p * TP + t * 128
                for half in range(2):
                    W("sync", "s_xtm", C["s_xtm"])
                    W("sync", "s_4ev", C["s_4ev"])
                    gxl[half] += 1
                    INC(f"s_x{half}", 16)
                    E("sync", lambda e, b=xh[half], rr=r0, hh=half:
                      e.dma_start(b[:, :],
                                  x_d[rr:rr + 128, hh * 2048:(hh + 1) * 2048])
                      .then_inc(sems[f"s_x{hh}"], 16))
                for u2 in range(2):
                    c0 = p * TP + t * 128 + u2 * 64
                    bufs = []
                    for mh in range(2):
                        par = (t * 4 + u2 * 2 + mh) % 2
                        bufs.append(par)
                        W("sync", "s_4tp", C["s_4tp"])
                        for m_ in range(mh * 16, mh * 16 + 16):
                            W("sync", f"s_ac{m_}", n_acc[m_])
                        W('sync', f"s_a4{par}", C[f"s_a4{par}"])
                        INC(f"s_a4{par}", 16)
                        E("sync", lambda e, cc=c0, pr=par, mm=mh:
                          e.dma_start(accb[pr][:, :, :],
                                      acc_dd[mm * 2048:(mm + 1) * 2048, cc:cc + 64]
                                      .rearrange("(mc p) t -> p mc t", p=128))
                          .then_inc(sems[f"s_a4{pr}"], 16))
                    for gg in range(8):
                        slot = gg % 2
                        par = bufs[gg // 4]
                        W("tensor", f"s_a4{par}", C[f"s_a4{par}"])
                        if C["s_4tp"] >= 2:
                            W("tensor", "s_4ev", C["s_4ev"] - 1)
                        for i in range(4):
                            mc = (gg * 4 + i) % 16
                            last = (i == 3)
                            if last:
                                INC("s_4tp")
                            def t4f(e, pr=par, mm=mc, sl=slot, ii=i, ll=last):
                                ins = e.transpose(
                                    ps[1][0:64, sl * 512 + ii * 128:
                                          sl * 512 + (ii + 1) * 128],
                                    accb[pr][:, mm, :], id_sb[:, :])
                                if ll:
                                    ins.then_inc(sems["s_4tp"], 1)
                            E("tensor", t4f)
                        W("vector", "s_4tp", C["s_4tp"])
                        W("vector", "s_x0", C["s_x0"])
                        W("vector", "s_x1", C["s_x1"])
                        W("vector", "s_out", C["s_out"])
                        INC("s_4ev")
                        E("vector", lambda e, uu=u2, gg_=gg, sl=slot:
                          e.tensor_tensor(
                              out_sb[uu * 64:(uu + 1) * 64,
                                     gg_ * 512:(gg_ + 1) * 512],
                              ps[1][0:64, sl * 512:(sl + 1) * 512],
                              xh[gg_ // 4][uu * 64:(uu + 1) * 64,
                                           (gg_ % 4) * 512:((gg_ % 4) + 1) * 512],
                              ALU.add)
                          .then_inc(sems["s_4ev"], 1))
                W("scalar", "s_4ev", C["s_4ev"])
                W('scalar', "s_out", C["s_out"])
                INC("s_out", 16)
                E("scalar", lambda e, rr=r0:
                  e.dma_start(out_d[rr:rr + 128, :], out_sb[:, :])
                  .then_inc(sems["s_out"], 16))
                if p + 1 < NPASS:
                    if t == 0:
                        emit_s1_guards(p + 1)
                    emit_s1_tile(p + 1, t)
        W("scalar", "s_out", C["s_out"])
        if debug:
            W("sync", "s_ini", C["s_ini"])

        with nc.Block() as blk:
            @blk.tensor
            def _(e):
                for f in prog["tensor"]:
                    f(e)

            @blk.vector
            def _(e):
                for f in prog["vector"]:
                    f(e)

            @blk.scalar
            def _(e):
                for f in prog["scalar"]:
                    f(e)

            @blk.gpsimd
            def _(e):
                for f in prog["gpsimd"]:
                    f(e)

            @blk.sync
            def _(e):
                for f in prog["sync"]:
                    f(e)
    return nc


def _prep_weights(inputs):
    f32 = lambda a: np.ascontiguousarray(np.asarray(a, dtype=np.float32))
    up = f32(inputs["L0_attn_up_w"])
    gate = f32(inputs["L0_attn_gate_w"])
    lp = f32(inputs["L0_attn_local_point"])
    lao = f32(inputs["L0_attn_local_approx_output"])
    fuse = f32(inputs["L0_attn_fuse_weight"])
    df = f32(inputs["L0_attn_discount_factor"])
    fg = f32(inputs["L0_attn_ffn_gate_w"])
    fu = f32(inputs["L0_attn_ffn_up_w"])
    fd = f32(inputs["L0_attn_ffn_down_w"])
    Wb = f32(inputs["L0_ffn_W_base"])
    bb = f32(inputs["L0_ffn_b_base"])
    Wg = f32(inputs["L0_ffn_W_gate"])
    A = f32(inputs["L0_ffn_A"])
    B = f32(inputs["L0_ffn_B"])

    p1cols = np.concatenate([up.T, gate.T, A.reshape(128, H).T], axis=1)
    proj1 = np.ascontiguousarray(
        p1cols.reshape(2, 16, 128, 17, 128).transpose(3, 0, 2, 1, 4))
    wg2 = np.ascontiguousarray(
        Wg.T.reshape(2, 16, 128, 8).transpose(0, 2, 1, 3))
    guc = np.concatenate(
        [np.ascontiguousarray(fg.T).reshape(H, FC, 128),
         np.ascontiguousarray(fu.T).reshape(H, FC, 128)], axis=2)
    gu = np.ascontiguousarray(
        guc.reshape(4, 8, 128, FC, 256).transpose(3, 0, 2, 1, 4))
    downT = np.ascontiguousarray(fd.T)
    laoT = np.ascontiguousarray(lao.T)
    fwT = [np.ascontiguousarray(fuse[:, :, m].T) for m in range(6)]
    WbT = np.ascontiguousarray(Wb.T)
    BT = np.ascontiguousarray(B.transpose(0, 2, 1).reshape(128, H))
    blocks = []
    for h in range(NH):
        j0 = sum(JG[:h])
        blocks.append(downT[j0 * 128:(j0 + JG[h]) * 128])
        blocks.append(laoT[h * 128:(h + 1) * 128])
        for m in range(6):
            blocks.append(fwT[m][h * 128:(h + 1) * 128])
        blocks.append(WbT[h * 512:(h + 1) * 512])
        if h == 0:
            blocks.append(BT)
    outw_rows = np.concatenate(blocks, axis=0)
    assert outw_rows.shape[0] == ROWOFF[-1]
    parts = []
    for h in range(NH):
        rh = RC_H[h]
        blk = outw_rows[ROWOFF[h] // 1:][:rh * 128]
        blk = outw_rows[ROWOFF[h]:ROWOFF[h] + rh * 128]
        b = blk.reshape(rh, 128, 32, 128).transpose(2, 1, 0, 3)
        parts.append(np.ascontiguousarray(b).reshape(-1))
    outw = np.ascontiguousarray(np.concatenate(parts))
    bt = np.ascontiguousarray(bb.reshape(32, 128).T)
    lpt = np.ascontiguousarray(lp.reshape(8, 128).T)
    p8 = np.kron(np.eye(8, dtype=np.float32), np.ones((1, 16), np.float32))
    ident = np.eye(128, dtype=np.float32)
    return dict(proj1=proj1, wg2=wg2, gu=gu, outw=outw, bt=bt, lpt=lpt,
                df=np.ascontiguousarray(df.reshape(1, 6)), p8=p8, ident=ident)


def kernel(**inputs):
    x = np.asarray(inputs["L0_attn_hidden_states"], dtype=np.float32)
    xf = np.ascontiguousarray(x.reshape(NCORE * TCORE, H))
    wmap = _prep_weights(inputs)
    nc = _build_program()
    in_maps = [dict(wmap, x=np.ascontiguousarray(xf[c * TCORE:(c + 1) * TCORE]))
               for c in range(NCORE)]
    res = run_bass_kernel_spmd(nc, in_maps, core_ids=list(range(NCORE)))
    out = np.concatenate([np.asarray(r["out"]) for r in res.results], axis=0)
    return out.reshape(x.shape).astype(np.float32)


if __name__ == "__main__":
    rng = np.random.default_rng(0)
    fake = {
        "L0_attn_hidden_states": rng.standard_normal((2, 8192, 4096), dtype=np.float32),
    }
    print("smoke build only")
    _build_program()
    print("build ok")
